# version 03 (final)
# baseline (speedup 1.0000x reference)
import math
import numpy as np
import ml_dtypes
from contextlib import ExitStack

import concourse.bass as bass
import concourse.mybir as mybir
from concourse.bass_utils import run_bass_kernel_spmd

F32 = mybir.dt.float32
BF16 = mybir.dt.bfloat16
AF = mybir.ActivationFunctionType
ALU = mybir.AluOpType

NCORES = 8
SEQ_PER_CORE = 4
T = 2048
D = 1024
NH = 16
HD = 64
NCH = D // 128
NTC = T // 128
FREQS = [31.0, 37.0, 41.0, 43.0, 47.0]
NF = 5
M_MAGIC = 12582912.0
TWO_PI_S = 6.28318
INV_2PI = float(1.0 / (2.0 * math.pi))
LN_EPS = 1e-5
ATTN_EPS = 1e-10
IB = 256
NIB = T // IB
WIN = 128
NW = T // WIN
GLB = 64

ENGS = ("sync", "tensor", "vector", "scalar", "gpsimd")


class Reg:
    __slots__ = ("name", "last_w", "readers")

    def __init__(self, name):
        self.name = name
        self.last_w = None
        self.readers = []


class Plan:
    def __init__(self, nc, es):
        self.nc = nc
        self.ops = {e: [] for e in ENGS}
        self.sems = {}
        for e in ENGS:
            self.sems[e] = es.enter_context(nc.semaphore(f"sem_{e}"))
        for d in ("din", "dwt", "dgb", "dbv", "dprep", "dtr", "dout"):
            self.sems[d] = es.enter_context(nc.semaphore(f"sem_{d}"))
        self.counts = {k: 0 for k in self.sems}
        self.floors = {e: {} for e in ENGS}

    def op(self, eng, emit, reads=(), writes=(), dma=None):
        waits = {}

        def add(tok):
            if tok is None:
                return
            sk, v = tok
            if sk == eng:
                return
            if v > waits.get(sk, 0):
                waits[sk] = v

        for r in reads:
            add(r.last_w)
        for w in writes:
            add(w.last_w)
            for t in w.readers:
                add(t)
        fl = self.floors[eng]
        waits = {sk: v for sk, v in waits.items() if v > fl.get(sk, 0)}
        for sk, v in waits.items():
            fl[sk] = v
        if emit is None:
            self.ops[eng].append((waits, None, None))
            return None
        if dma is None:
            self.counts[eng] += 1
            tok = (eng, self.counts[eng])
        else:
            self.counts[dma] += 16
            tok = (dma, self.counts[dma])
        self.ops[eng].append((waits, emit, tok))
        for w in writes:
            w.last_w = tok
            w.readers = []
        for r in reads:
            r.readers.append(tok)
        return tok

    def dma(self, eng, out_ap, in_ap, writes, reads=(), sem="din"):
        return self.op(eng, lambda e: e.dma_start(out=out_ap, in_=in_ap),
                       reads=list(reads), writes=list(writes), dma=sem)

    def wait_only(self, eng, toks):
        waits = {}
        for tok in toks:
            if tok is None:
                continue
            sk, v = tok
            if v > waits.get(sk, 0):
                waits[sk] = v
        self.ops[eng].append((waits, None, None))

    def build_block(self):
        nc = self.nc
        plan = self
        used = set()
        for ename in ENGS:
            for waits, _, _ in plan.ops[ename]:
                for sk, v in waits.items():
                    used.add((sk, v))

        def mk(ename):
            def body(eng):
                for waits, emit, tok in plan.ops[ename]:
                    for sk, v in waits.items():
                        eng.wait_ge(plan.sems[sk], v)
                    if emit is None:
                        continue
                    inst = emit(eng)
                    sk, v = tok
                    if sk in ENGS:
                        eng.drain(fusable=False).then_inc(plan.sems[sk], 1)
                    else:
                        inst.then_inc(plan.sems[sk], 16)
            return body

        with nc.Block() as block:
            for ename in ENGS:
                if self.ops[ename]:
                    getattr(block, ename)(mk(ename))


def build_module(dev=False):
    nc = bass.Bass("TRN2", target_bir_lowering=False, detect_race_conditions=False)

    def din(name, shape, dtype=F32):
        return nc.dram_tensor(name, list(shape), dtype, kind="ExternalInput").ap()

    x_d = din("x", [SEQ_PER_CORE, T, D])
    wqT_d = din("wqT", [D, D]); wkT_d = din("wkT", [D, D])
    wvT_d = din("wvT", [D, D]); woT_d = din("woT", [D, D])
    bq_d = din("bq_p", [128, NCH]); bk_d = din("bk_p", [128, NCH])
    bv_d = din("bv_row", [1, D]); bo_d = din("bo_row", [1, D])
    alpha_d = din("alpha_t", [128, NH * NF]); phi_d = din("phi_t", [128, NH * NF])
    temp_d = din("temp_t", [128, 1])
    gamma_d = din("gamma_t", [128, D]); beta_d = din("beta_t", [128, D])
    eye_d = din("eye_bf", [128, 128], BF16)
    eyef_d = din("eye_f", [128, 128])
    f0r_d = din("f0rT", [D, D]); f0k_d = din("f0kT", [D, D]); f0v_d = din("f0vT", [D, D])
    f1r_d = din("f1rT", [D, D]); f1k_d = din("f1kT", [D, D]); f1v_d = din("f1vT", [D, D])
    mu_d = din("mu_p", [128, 4 * NCH])
    w1qT_d = din("w1qT", [D, D]); w1kT_d = din("w1kT", [D, D])
    w1vT_d = din("w1vT", [D, D]); w1oT_d = din("w1oT", [D, D])

    out_d = nc.dram_tensor("out", [SEQ_PER_CORE, T, D], F32, kind="ExternalOutput").ap()
    hkind = "ExternalOutput" if dev else "Internal"
    h1_d = nc.dram_tensor("h1", [SEQ_PER_CORE, T, D], F32, kind=hkind).ap()
    h2_d = nc.dram_tensor("h2", [SEQ_PER_CORE, T, D], F32, kind=hkind).ap()
    h3_d = nc.dram_tensor("h3", [SEQ_PER_CORE, T, D], F32, kind=hkind).ap()

    es = ExitStack()
    with es:
        p = Plan(nc, es)
        sb = lambda name, shape, dt: es.enter_context(nc.sbuf_tensor(name, shape, dt))
        ps = lambda name, shape, dt: es.enter_context(nc.psum_tensor(name, shape, dt))

        ones_f = sb("c_ones_f", [128, 1], F32)
        eps_t = sb("c_eps", [128, 1], F32)
        ones1_bf = sb("c_ones1bf", [1, 128], BF16)
        ones_col_bf = sb("c_onescolbf", [128, 1], BF16)
        ones1_f32 = sb("c_ones1f", [1, 128], F32)
        r_ones = Reg("consts")
        p.op("vector", lambda e: e.memset(ones_f[:], 1.0), writes=[r_ones])
        p.op("vector", lambda e: e.memset(eps_t[:], LN_EPS), writes=[r_ones])
        p.op("vector", lambda e: e.memset(ones1_bf[:], 1.0), writes=[r_ones])
        p.op("vector", lambda e: e.memset(ones_col_bf[:], 1.0), writes=[r_ones])
        p.op("vector", lambda e: e.memset(ones1_f32[:], 1.0), writes=[r_ones])

        xt = sb("xt", [128, D], F32)
        r_xt = Reg("xt0")
        xnT = sb("xnT", [128, NCH * T], BF16)
        r_xnT = Reg("xnT")
        wstage = sb("wstage", [128, NCH * D], F32)
        r_wstage = Reg("wstage")
        w_bf = sb("w_bf", [128, NCH * D], BF16)
        r_wbf = Reg("w_bf")
        QT = sb("QT", [128, NCH * T], BF16)
        r_QT = Reg("QT")
        KT = sb("KT", [128, NCH * T], BF16)
        r_KT = Reg("KT")
        V = sb("V", [128, NTC * D], BF16)
        r_V = Reg("V")
        u0 = xnT[:].bitcast(F32)
        r_u0 = [Reg("u0_0"), Reg("u0_1")]
        scores = wstage[:, 0:4096]
        r_scores = Reg("scores")
        ewA = wstage[:, 4096:6144]
        r_ewA = Reg("ewA")
        ewB = wstage[:, 6144:8192]
        r_ewB = Reg("ewB")
        w_att = sb("w_att", [128, 16 * IB], BF16)
        r_watt = Reg("watt")
        xn_sl = [w_att[:, 0:D], w_att[:, D:2 * D]]
        r_xn_sl = [Reg("xna"), Reg("xnb")]
        sink_t = sb("sink_t", [128, D], BF16)
        st_sink = sink_t[:]
        r_sink = Reg("sink")
        attnT = sb("attnT", [128, 8 * IB], BF16)
        r_attnT = Reg("attnT")
        y0 = sb("y0", [128, D], F32)
        r_y0 = Reg("y0")
        recip_t = y0[:, 0:2 * IB]
        r_recip = r_y0
        bc_sb = y0[:, 2 * IB:3 * IB]
        r_bc = r_y0
        muvar = sb("muvar", [128, 4], F32)
        r_muvar = Reg("muvar")
        rstd_t = sb("rstd_t", [128, 2], F32)
        r_rstd = Reg("rstd")
        small = sb("small", [128, 8], F32)
        r_small = Reg("small")
        r_flush = Reg("flush")
        gamma_f = ewA[:, 0:D]
        beta_f = ewA[:, D:2 * D]
        r_gb = r_ewA
        alpha_t = sb("alpha_sb", [128, NH * NF], F32)
        phi_t = sb("phi_sb", [128, NH * NF], F32)
        phiM_t = sb("phiM_sb", [128, NH * NF], F32)
        r_aphi = Reg("aphi")
        temp_t = sb("temp_sb", [128, 1], F32)
        rtemp8 = sb("rtemp8", [128, 1], F32)
        r_temp = Reg("temp")
        bq_t = sb("bq_sb", [128, NCH], F32)
        bk_t = sb("bk_sb", [128, NCH], F32)
        r_bqk = Reg("bqk")
        mu_t = sb("mu_sb", [128, 4 * NCH], F32)
        r_mu = Reg("mu")
        bo_bf = sb("bo_bf", [1, D], BF16)
        r_bvo = Reg("bvo")
        eye_bf = sb("eye_sb", [128, 128], BF16)
        eye_f = sb("eyef_sb", [128, 128], F32)
        r_eye = Reg("eye")

        wr_bf = QT[:, 0:NCH * D]
        dfT = QT[:, NCH * D:NCH * D + 4096]
        wk2_bf = KT[:, 0:NCH * D]
        wv2_bf = V[:, 0:NCH * D]
        xrT = V[:, NCH * D:NCH * D + 4096]
        xkT = KT[:, NCH * D:NCH * D + 4096]
        kactT = w_att[:, 0:4096]
        r_ffn_sb = ewA[:, 0:D]
        p_raw = w_att[:, 3 * D:3 * D + 192]
        pT_g = w_att[:, 3 * D + 384:3 * D + 512]
        pT_w = w_att[:, 3 * D + 512:3 * D + 640]

        arena = ps("arena", [128, 3584], F32)
        pbf = ps("pbf", [128, 1024], BF16)
        sim = [arena[:, 0:1024], arena[:, 1024:2048]]
        r_sim = [Reg("sim0"), Reg("sim1")]
        av_ps = arena[:, 2048:2560]
        r_av = Reg("av_ps")
        nm_ps = arena[:, 2560:3072]
        r_nm = Reg("nm_ps")
        bc_ps = arena[:, 3072:3584]
        r_bcps = Reg("bc_ps")
        pq_ps = arena[:, 0:512]
        r_pq = r_sim[0]
        po_ps = arena[:, 2048:2560]
        r_po = r_av
        r_pbf = Reg("pbf")

        def dve_flush(extra_writes=()):
            p.op("vector", lambda e: e.tensor_copy(small[:, 7:8], small[:, 7:8]),
                 reads=[], writes=[r_flush] + list(extra_writes))

        r_warm = Reg("warm")
        warm = small[:, 0:1]
        p.dma("sync", eye_bf[:], eye_d[:], [r_eye], sem="dprep")
        p.dma("sync", eye_f[:], eyef_d[:], [r_eye], sem="dprep")
        p.op("vector", lambda e: e.memset(small[:], 1.0),
             writes=[r_warm, r_small, r_flush])
        for fn in (AF.Identity, AF.Square, AF.Sqrt, AF.Sin, AF.Sigmoid, AF.Exp):
            p.op("scalar", lambda e, fn=fn: e.activation(
                    out=small[:, 1:2], in_=warm, func=fn, scale=0.1, bias=0.0),
                 reads=[r_warm], writes=[r_warm])
        p.op("vector", lambda e: e.tensor_scalar(out=small[:, 1:2], in0=warm,
                                                 scalar1=1.0, scalar2=0.0,
                                                 op0=ALU.mult, op1=ALU.add),
             reads=[r_warm], writes=[r_warm])
        p.op("gpsimd", lambda e: e.tensor_scalar(out=small[:, 2:3], in0=warm,
                                                 scalar1=1.0, scalar2=0.0,
                                                 op0=ALU.mult, op1=ALU.add),
             reads=[r_warm], writes=[r_warm])
        p.op("tensor", lambda e: e.matmul(out=arena[:, 0:128], lhsT=eye_bf[:],
                                          rhs=eye_bf[:], start=True, stop=True),
             reads=[r_eye], writes=[r_sim[0], r_warm])
        for ename in ("vector", "scalar", "gpsimd", "tensor", "sync"):
            p.op(ename, None, reads=[r_warm])

        p.dma("sync", alpha_t[:], alpha_d[:], [r_aphi], sem="dprep")
        p.dma("sync", phi_t[:], phi_d[:], [r_aphi], sem="dprep")
        p.dma("sync", temp_t[:], temp_d[:], [r_temp], sem="dprep")
        p.dma("sync", bq_t[:], bq_d[:], [r_bqk], sem="dprep")
        p.dma("sync", bk_t[:], bk_d[:], [r_bqk], sem="dprep")
        p.dma("sync", mu_t[:], mu_d[:], [r_mu], sem="dprep")
        p.dma("sync", y0[0:1, 0:D], bo_d[0:1, :], [r_y0], sem="dprep")
        _prep_tok = ("dprep", p.counts["dprep"])
        for _r in (r_aphi, r_temp, r_bqk, r_eye, r_y0, r_mu):
            _r.last_w = _prep_tok

        p.op("vector", lambda e: e.tensor_scalar(out=phiM_t[:], in0=phi_t[:],
                                                 scalar1=INV_2PI, scalar2=0.0,
                                                 op0=ALU.mult, op1=ALU.add),
             reads=[r_aphi], writes=[r_aphi])
        p.op("scalar", lambda e: e.activation(out=small[:, 2:3], in_=temp_t[:],
                                              func=AF.Abs, scale=1.0, bias=0.0),
             reads=[r_temp], writes=[r_small])
        p.op("vector", lambda e: e.tensor_scalar(out=small[:, 3:4], in0=small[:, 2:3],
                                                 scalar1=0.1, scalar2=8.0,
                                                 op0=ALU.add, op1=ALU.mult),
             reads=[r_small], writes=[r_small])
        p.op("vector", lambda e: e.reciprocal(out=rtemp8[:], in_=small[:, 3:4]),
             reads=[r_small], writes=[r_temp])
        p.op("vector", lambda e: e.tensor_copy(bo_bf[0:1, 0:D], y0[0:1, 0:D]),
             reads=[r_y0], writes=[r_bvo])

        def ln_stats(src_ap, r_src, col):
            def em_st(e, src_ap=src_ap):
                e.activation(out=st_sink, in_=src_ap, func=AF.Identity,
                             scale=1.0, bias=0.0, accum_out=small[:, 4:5])
                return e.activation(out=st_sink, in_=src_ap, func=AF.Square,
                                    scale=1.0, bias=0.0, accum_out=small[:, 5:6])
            p.op("scalar", em_st, reads=[r_src], writes=[r_sink, r_small])

            def em_mv(e, col=col):
                e.tensor_scalar(out=muvar[:, 2 * col:2 * col + 1], in0=small[:, 4:5],
                                scalar1=1.0 / D, scalar2=0.0,
                                op0=ALU.mult, op1=ALU.add)
                e.drain(fusable=False)
                e.tensor_scalar(out=small[:, 6:7],
                                in0=muvar[:, 2 * col:2 * col + 1],
                                scalar1=muvar[:, 2 * col:2 * col + 1], scalar2=0.0,
                                op0=ALU.mult, op1=ALU.add)
                e.drain(fusable=False)
                return e.scalar_tensor_tensor(
                    out=muvar[:, 2 * col + 1:2 * col + 2], in0=small[:, 5:6],
                    scalar=1.0 / D, in1=small[:, 6:7],
                    op0=ALU.mult, op1=ALU.subtract)
            p.op("vector", em_mv, reads=[r_small], writes=[r_muvar, r_small])

        def rstd_calc(col):
            var_ap = muvar[:, 2 * col + 1:2 * col + 2]
            p.op("scalar",
                 lambda e: e.activation(out=small[:, col:col + 1], in_=var_ap,
                                        func=AF.Sqrt, scale=1.0, bias=eps_t[:, 0:1]),
                 reads=[r_muvar, r_ones], writes=[r_small])
            p.op("vector",
                 lambda e: e.reciprocal(out=rstd_t[:, col:col + 1],
                                        in_=small[:, col:col + 1]),
                 reads=[r_small], writes=[r_rstd])

        def emit_preln(s, src_d, first_block):
            def emit_transposes(tcx, extra_reads):
                slot_ap = xn_sl[tcx % 2]
                slot_reg = r_xn_sl[tcx % 2]
                for cc in range(NCH):
                    half = cc % 2
                    pb_ap = pbf[:, half * 128:(half + 1) * 128]

                    def em_tr(e, cc=cc, slot_ap=slot_ap, pb_ap=pb_ap):
                        return e.transpose(out=pb_ap,
                                           in_=slot_ap[:, cc * 128:(cc + 1) * 128],
                                           identity=eye_bf[:])
                    p.op("tensor", em_tr, reads=[slot_reg, r_eye] + extra_reads,
                         writes=[r_pbf])
                    wt = [r_xnT]
                    if tcx == 0 and cc == 0 and first_block:
                        wt += [r_u0[0], r_u0[1]]

                    def em_cp(e, cc=cc, tcx=tcx, pb_ap=pb_ap):
                        return e.tensor_copy(
                            xnT[:, cc * T + tcx * 128:cc * T + (tcx + 1) * 128],
                            pb_ap)
                    p.op("vector", em_cp, reads=[r_pbf], writes=wt)

            for tc in range(NTC):
                xt_ap = xt[:, 0:D]
                p.dma("sync", xt_ap, src_d[s, tc * 128:(tc + 1) * 128, :], [r_xt])
                ln_stats(xt_ap, r_xt, 0)
                rstd_calc(0)

                def em_norm(e, xa=xt_ap, xo=xn_sl[tc % 2]):
                    return e.tensor_scalar(out=xo, in0=xa, scalar1=muvar[:, 0:1],
                                           scalar2=rstd_t[:, 0:1],
                                           op0=ALU.subtract, op1=ALU.mult)
                p.op("vector", em_norm, reads=[r_xt, r_muvar, r_rstd],
                     writes=[r_xn_sl[tc % 2]])
                if tc >= 1:
                    emit_transposes(tc - 1, [r_xn_sl[tc % 2]])
            dve_flush()
            dve_flush(extra_writes=[r_xn_sl[15 % 2]])
            emit_transposes(15, [r_flush])

        def load_w(wd):
            p.dma("sync", wstage.rearrange("p (c d) -> p c d", c=NCH),
                  wd.rearrange("(c p) d -> p c d", p=128),
                  [r_wstage, r_scores, r_ewA, r_ewB], sem="dwt")
            p.op("vector", lambda e: e.tensor_copy(w_bf[:], wstage[:]),
                 reads=[r_wstage, r_scores, r_ewA, r_ewB], writes=[r_wbf])

        def load_w_to(wd, dst_ap, r_dst):
            p.dma("sync", wstage.rearrange("p (c d) -> p c d", c=NCH),
                  wd.rearrange("(c p) d -> p c d", p=128),
                  [r_wstage, r_scores, r_ewA, r_ewB], sem="dwt")
            p.op("vector", lambda e: e.tensor_copy(dst_ap, wstage[:]),
                 reads=[r_wstage, r_scores, r_ewA, r_ewB], writes=[r_dst])

        def emit_tail(s, tc, src_d, dst_d):
            ln_stats(y0[:], r_y0, 1)
            rstd_calc(1)
            xt_ap = xt[:, 0:D]
            p.dma("sync", xt_ap, src_d[s, tc * 128:(tc + 1) * 128, :], [r_xt])
            p.op("vector", lambda e: e.tensor_scalar(
                    out=y0[:], in0=y0[:], scalar1=muvar[:, 2:3],
                    scalar2=rstd_t[:, 1:2], op0=ALU.subtract, op1=ALU.mult),
                 reads=[r_muvar, r_rstd], writes=[r_y0])

            def em_res(e, xa=xt_ap):
                return e.tensor_tensor(out=y0[:], in0=y0[:], in1=xa, op=ALU.add)
            p.op("vector", em_res, reads=[r_xt], writes=[r_y0])
            dve_flush(extra_writes=[r_y0])
            p.dma("gpsimd", dst_d[s, tc * 128:(tc + 1) * 128, :], y0[:],
                  [Reg(f"o{s}_{tc}")], reads=[r_y0], sem="dout")

        def emit_l0attn(s, src_d, dst_d):
            emit_preln(s, src_d, first_block=(s == 0))
            for wd, bias_t, scale_q, dst, r_dst in (
                    (wqT_d, bq_t, True, QT, r_QT), (wkT_d, bk_t, False, KT, r_KT)):
                load_w(wd)
                for dc in range(NCH):
                    for tb in range(4):
                        def em_mm(e, dc=dc, tb=tb):
                            last = None
                            for cc in range(NCH):
                                last = e.matmul(
                                    out=pq_ps[:, 0:512],
                                    lhsT=w_bf[:, cc * D + dc * 128:cc * D + (dc + 1) * 128],
                                    rhs=xnT[:, cc * T + tb * 512:cc * T + (tb + 1) * 512],
                                    start=(cc == 0), stop=(cc == NCH - 1))
                            return last
                        p.op("tensor", em_mm, reads=[r_wbf, r_xnT], writes=[r_pq])

                        def em_ev(e, dc=dc, tb=tb, bias_t=bias_t, scale_q=scale_q,
                                  dst=dst):
                            s2 = rtemp8[:, 0:1] if scale_q else ones_f[:, 0:1]
                            return e.tensor_scalar(
                                out=dst[:, dc * T + tb * 512:dc * T + (tb + 1) * 512],
                                in0=pq_ps[:, 0:512],
                                scalar1=bias_t[:, dc:dc + 1], scalar2=s2,
                                op0=ALU.add, op1=ALU.mult)
                        p.op("vector", em_ev, reads=[r_pq, r_bqk, r_temp, r_ones],
                             writes=[r_dst])

            load_w(wvT_d)
            p.dma("sync", y0[0:1, 0:D], bv_d[0:1, :], [r_y0], sem="dbv")
            p.op("vector", lambda e: e.tensor_copy(attnT[0:1, 0:D], y0[0:1, 0:D]),
                 reads=[r_y0], writes=[r_attnT])
            for tc in range(NTC):
                for db in range(2):
                    def em_mm(e, tc=tc, db=db):
                        for cc in range(NCH):
                            e.matmul(out=pq_ps[:, 0:512],
                                     lhsT=xnT[:, cc * T + tc * 128:cc * T + (tc + 1) * 128],
                                     rhs=w_bf[:, cc * D + db * 512:cc * D + (db + 1) * 512],
                                     start=(cc == 0), stop=False)
                        return e.matmul(out=pq_ps[:, 0:512],
                                        lhsT=ones1_bf[0:1, 0:128],
                                        rhs=attnT[0:1, db * 512:(db + 1) * 512],
                                        start=False, stop=True)
                    p.op("tensor", em_mm, reads=[r_xnT, r_wbf, r_attnT, r_ones],
                         writes=[r_pq])

                    def em_ev(e, tc=tc, db=db):
                        return e.tensor_copy(
                            V[:, tc * D + db * 512:tc * D + (db + 1) * 512],
                            pq_ps[:, 0:512])
                    p.op("vector", em_ev, reads=[r_pq], writes=[r_V])

            load_w(woT_d)

            for ib in range(NIB):
                for pr in range(8):
                    for jj in range(4):
                        def em_qk(e, pr=pr, ib=ib, jj=jj):
                            last = None
                            for k in range(4):
                                jc = 4 * jj + k
                                for h in range(2):
                                    last = e.matmul(
                                        out=sim[h][:, k * IB:(k + 1) * IB],
                                        lhsT=KT[h * 64:(h + 1) * 64,
                                                pr * T + jc * 128:pr * T + (jc + 1) * 128],
                                        rhs=QT[h * 64:(h + 1) * 64,
                                               pr * T + ib * IB:pr * T + (ib + 1) * IB],
                                        start=True, stop=True,
                                        tile_position=(h * 64, 0))
                            return last
                        p.op("tensor", em_qk, reads=[r_QT, r_KT],
                             writes=[r_sim[0], r_sim[1]])
                        for h in range(2):
                            def em_evac(e, h=h, jj=jj):
                                return e.activation(
                                    out=u0[:, h * 4096 + jj * 1024:h * 4096 + (jj + 1) * 1024],
                                    in_=sim[h][:, 0:1024], func=AF.Identity,
                                    scale=1.0, bias=0.0)
                            p.op("scalar", em_evac, reads=[r_sim[h]],
                                 writes=[r_u0[h], r_xnT])

                    for h in range(2):
                        hg = 2 * pr + h
                        for f in range(NF):
                            inv_f = 1.0 / FREQS[f]
                            pcol = hg * NF + f
                            for sub in range(2):
                                u0_ap = u0[:, h * 4096 + sub * 2048:h * 4096 + (sub + 1) * 2048]

                                def em_d1(e, inv_f=inv_f, pcol=pcol, u0_ap=u0_ap):
                                    e.tensor_scalar(out=ewA, in0=u0_ap,
                                                    scalar1=inv_f,
                                                    scalar2=phiM_t[:, pcol:pcol + 1],
                                                    op0=ALU.mult, op1=ALU.add)
                                    e.drain(fusable=False)
                                    return e.tensor_scalar(out=ewB, in0=ewA,
                                                           scalar1=M_MAGIC, scalar2=M_MAGIC,
                                                           op0=ALU.add, op1=ALU.subtract)
                                p.op("vector", em_d1, reads=[r_u0[h], r_aphi],
                                     writes=[r_ewA, r_ewB])

                                def em_d2(e):
                                    return e.scalar_tensor_tensor(
                                        out=ewA, in0=ewB, scalar=-1.0, in1=ewA,
                                        op0=ALU.mult, op1=ALU.add)
                                p.op("vector", em_d2, reads=[r_ewB],
                                     writes=[r_ewA])

                                def em_sin(e):
                                    return e.activation(out=ewB, in_=ewA,
                                                        func=AF.Sin, scale=TWO_PI_S,
                                                        bias=0.0)
                                p.op("scalar", em_sin, reads=[r_ewA],
                                     writes=[r_ewB])

                                sc_ap = scores[:, sub * 2048:(sub + 1) * 2048]
                                if f == 0:
                                    def em_acc(e, pcol=pcol, sc_ap=sc_ap):
                                        return e.tensor_scalar(
                                            out=sc_ap, in0=ewB,
                                            scalar1=alpha_t[:, pcol:pcol + 1],
                                            scalar2=0.0, op0=ALU.mult, op1=ALU.add)
                                else:
                                    def em_acc(e, pcol=pcol, sc_ap=sc_ap):
                                        return e.scalar_tensor_tensor(
                                            out=sc_ap, in0=ewB,
                                            scalar=alpha_t[:, pcol:pcol + 1],
                                            in1=sc_ap, op0=ALU.mult, op1=ALU.add)
                                p.op("vector", em_acc,
                                     reads=[r_ewB, r_aphi, r_scores],
                                     writes=[r_scores])

                        def em_sig(e):
                            return e.activation(out=w_att[:], in_=scores,
                                                func=AF.Sigmoid, scale=1.0, bias=0.0)
                        p.op("scalar", em_sig, reads=[r_scores],
                             writes=[r_watt, r_xn_sl[0], r_xn_sl[1], r_sink])

                        def em_av(e, pr=pr, h=h):
                            last = None
                            for jc in range(NTC):
                                last = e.matmul(
                                    out=av_ps[h * 64:(h + 1) * 64, 0:IB],
                                    lhsT=V[:, jc * D + (2 * pr + h) * 64:
                                           jc * D + (2 * pr + h + 1) * 64],
                                    rhs=w_att[:, jc * IB:(jc + 1) * IB],
                                    start=(jc == 0), stop=(jc == NTC - 1),
                                    tile_position=(0, h * 64))
                            return last
                        p.op("tensor", em_av, reads=[r_V, r_watt], writes=[r_av])

                        def em_nm(e, h=h):
                            last = None
                            for jc in range(NTC):
                                last = e.matmul(
                                    out=nm_ps[h * 32:h * 32 + 1, 0:IB],
                                    lhsT=ones_col_bf[:, 0:1],
                                    rhs=w_att[:, jc * IB:(jc + 1) * IB],
                                    start=(jc == 0), stop=(jc == NTC - 1),
                                    tile_position=(0, h * 32))
                            return last
                        p.op("tensor", em_nm, reads=[r_ones, r_watt], writes=[r_nm])

                        def em_rc(e, h=h):
                            e.tensor_scalar(out=recip_t[32:33, h * IB:(h + 1) * IB],
                                            in0=nm_ps[h * 32:h * 32 + 1, 0:IB],
                                            scalar1=ATTN_EPS, scalar2=0.0,
                                            op0=ALU.add, op1=ALU.add)
                            e.drain(fusable=False)
                            return e.reciprocal(
                                out=recip_t[0:1, h * IB:(h + 1) * IB],
                                in_=recip_t[32:33, h * IB:(h + 1) * IB])
                        p.op("vector", em_rc, reads=[r_nm], writes=[r_recip])

                        def em_bc(e, h=h):
                            return e.matmul(out=bc_ps[h * 64:(h + 1) * 64, 0:IB],
                                            lhsT=ones1_f32[0:1, 0:64],
                                            rhs=recip_t[0:1, h * IB:(h + 1) * IB],
                                            start=True, stop=True,
                                            tile_position=(0, h * 64))
                        p.op("tensor", em_bc, reads=[r_recip, r_ones],
                             writes=[r_bcps])

                    def em_bcv(e):
                        return e.tensor_copy(bc_sb, bc_ps[:, 0:IB])
                    p.op("vector", em_bcv, reads=[r_bcps], writes=[r_bc])

                    def em_div(e, pr=pr):
                        return e.tensor_tensor(out=attnT[:, pr * IB:(pr + 1) * IB],
                                               in0=av_ps[:, 0:IB], in1=bc_sb,
                                               op=ALU.mult)
                    p.op("vector", em_div, reads=[r_av, r_bc], writes=[r_attnT])

                p.dma("sync", gamma_f, gamma_d[:], [r_gb], sem="dgb")
                p.dma("sync", beta_f, beta_d[:], [r_gb], sem="dgb")
                for tt in range(IB // 128):
                    tc = ib * (IB // 128) + tt
                    for db in range(2):
                        def em_mm(e, tt=tt, db=db):
                            for pp in range(8):
                                e.matmul(out=po_ps[:, 0:512],
                                         lhsT=attnT[:, pp * IB + tt * 128:
                                                    pp * IB + (tt + 1) * 128],
                                         rhs=w_bf[:, pp * D + db * 512:
                                                  pp * D + (db + 1) * 512],
                                         start=(pp == 0), stop=False)
                            return e.matmul(out=po_ps[:, 0:512],
                                            lhsT=ones1_bf[0:1, 0:128],
                                            rhs=bo_bf[0:1, db * 512:(db + 1) * 512],
                                            start=False, stop=True)
                        p.op("tensor", em_mm,
                             reads=[r_attnT, r_wbf, r_bvo, r_ones], writes=[r_po])

                        def em_ev(e, db=db):
                            return e.tensor_copy(y0[:, db * 512:(db + 1) * 512],
                                                 po_ps[:, 0:512])
                        p.op("vector", em_ev, reads=[r_po], writes=[r_y0])

                    ln_stats(y0[:], r_y0, 0)
                    rstd_calc(0)
                    p.op("vector", lambda e: e.tensor_scalar(
                            out=y0[:], in0=y0[:], scalar1=muvar[:, 0:1],
                            scalar2=rstd_t[:, 0:1], op0=ALU.subtract, op1=ALU.mult),
                         reads=[r_muvar, r_rstd], writes=[r_y0])
                    p.op("vector", lambda e: e.tensor_tensor(out=y0[:], in0=y0[:],
                                                             in1=gamma_f,
                                                             op=ALU.mult),
                         reads=[r_gb], writes=[r_y0])
                    p.op("vector", lambda e: e.tensor_tensor(out=y0[:], in0=y0[:],
                                                             in1=beta_f,
                                                             op=ALU.add),
                         reads=[r_gb], writes=[r_y0])
                    emit_tail(s, tc, src_d, dst_d)

        def emit_ffn(s, src_d, dst_d, frT_d, fkT_d, fvT_d, mu_base):
            emit_preln(s, src_d, first_block=False)
            load_w_to(frT_d, wr_bf, r_QT)
            load_w_to(fkT_d, wk2_bf, r_KT)
            load_w_to(fvT_d, wv2_bf, r_V)
            mur = mu_t[:, mu_base * NCH:(mu_base + 1) * NCH]
            muk = mu_t[:, (mu_base + 1) * NCH:(mu_base + 2) * NCH]

            for tb in range(4):
                for cc in range(NCH):
                    cur = xnT[:, cc * T + tb * 512:cc * T + (tb + 1) * 512]
                    if tb == 0:
                        prv = xnT[:, cc * T:cc * T + 511]

                        def em_df(e, cc=cc, cur=cur, prv=prv):
                            e.tensor_copy(dfT[:, cc * 512:cc * 512 + 1],
                                          cur[:, 0:1])
                            return e.tensor_tensor(
                                out=dfT[:, cc * 512 + 1:(cc + 1) * 512],
                                in0=cur[:, 1:512], in1=prv, op=ALU.subtract)
                        prv_full = xnT[:, cc * T:cc * T + 512]
                    else:
                        prv = xnT[:, cc * T + tb * 512 - 1:cc * T + (tb + 1) * 512 - 1]

                        def em_df(e, cc=cc, cur=cur, prv=prv):
                            return e.tensor_tensor(
                                out=dfT[:, cc * 512:(cc + 1) * 512],
                                in0=cur, in1=prv, op=ALU.subtract)
                        prv_full = prv
                    p.op("vector", em_df, reads=[r_xnT], writes=[r_QT])

                    def em_xr(e, cc=cc, prv_full=prv_full, tb=tb):
                        df = dfT[:, cc * 512:(cc + 1) * 512]
                        if tb == 0:
                            e.tensor_scalar(out=xrT[:, cc * 512:cc * 512 + 1],
                                            in0=df[:, 0:1],
                                            scalar1=mur[:, cc:cc + 1], scalar2=0.0,
                                            op0=ALU.mult, op1=ALU.add)
                            return e.scalar_tensor_tensor(
                                out=xrT[:, cc * 512 + 1:(cc + 1) * 512],
                                in0=df[:, 1:512], scalar=mur[:, cc:cc + 1],
                                in1=prv_full[:, 0:511], op0=ALU.mult, op1=ALU.add)
                        return e.scalar_tensor_tensor(
                            out=xrT[:, cc * 512:(cc + 1) * 512],
                            in0=df, scalar=mur[:, cc:cc + 1],
                            in1=prv_full, op0=ALU.mult, op1=ALU.add)
                    p.op("vector", em_xr, reads=[r_QT, r_xnT, r_mu], writes=[r_V])

                    def em_xk(e, cc=cc, prv_full=prv_full, tb=tb):
                        df = dfT[:, cc * 512:(cc + 1) * 512]
                        if tb == 0:
                            e.tensor_scalar(out=xkT[:, cc * 512:cc * 512 + 1],
                                            in0=df[:, 0:1],
                                            scalar1=muk[:, cc:cc + 1], scalar2=0.0,
                                            op0=ALU.mult, op1=ALU.add)
                            return e.scalar_tensor_tensor(
                                out=xkT[:, cc * 512 + 1:(cc + 1) * 512],
                                in0=df[:, 1:512], scalar=muk[:, cc:cc + 1],
                                in1=prv_full[:, 0:511], op0=ALU.mult, op1=ALU.add)
                        return e.scalar_tensor_tensor(
                            out=xkT[:, cc * 512:(cc + 1) * 512],
                            in0=df, scalar=muk[:, cc:cc + 1],
                            in1=prv_full, op0=ALU.mult, op1=ALU.add)
                    p.op("vector", em_xk, reads=[r_QT, r_xnT, r_mu], writes=[r_KT])

                for dc in range(NCH):
                    def em_kmm(e, dc=dc):
                        last = None
                        for cc in range(NCH):
                            last = e.matmul(
                                out=pq_ps[:, 0:512],
                                lhsT=wk2_bf[:, cc * D + dc * 128:cc * D + (dc + 1) * 128],
                                rhs=xkT[:, cc * 512:(cc + 1) * 512],
                                start=(cc == 0), stop=(cc == NCH - 1))
                        return last
                    p.op("tensor", em_kmm, reads=[r_KT], writes=[r_pq])

                    def em_krelu(e, dc=dc):
                        return e.activation(out=attnT[:, 0:512],
                                            in_=pq_ps[:, 0:512], func=AF.Relu,
                                            scale=1.0, bias=0.0)
                    p.op("scalar", em_krelu, reads=[r_pq], writes=[r_attnT])

                    def em_ksq(e, dc=dc):
                        a = attnT[:, 0:512]
                        return e.tensor_tensor(out=kactT[:, dc * 512:(dc + 1) * 512],
                                               in0=a, in1=a, op=ALU.mult)
                    p.op("vector", em_ksq, reads=[r_attnT],
                         writes=[r_watt, r_xn_sl[0], r_xn_sl[1]])

                for tt in range(4):
                    tc = tb * 4 + tt
                    for db in range(2):
                        def em_rmm(e, tt=tt, db=db):
                            return_last = None
                            for cc in range(NCH):
                                return_last = e.matmul(
                                    out=po_ps[:, 0:512],
                                    lhsT=xrT[:, cc * 512 + tt * 128:cc * 512 + (tt + 1) * 128],
                                    rhs=wr_bf[:, cc * D + db * 512:cc * D + (db + 1) * 512],
                                    start=(cc == 0), stop=(cc == NCH - 1))
                            return return_last
                        p.op("tensor", em_rmm, reads=[r_V, r_QT], writes=[r_po])

                        def em_rsig(e, db=db):
                            return e.activation(out=r_ffn_sb[:, db * 512:(db + 1) * 512],
                                                in_=po_ps[:, 0:512], func=AF.Sigmoid,
                                                scale=1.0, bias=0.0)
                        p.op("scalar", em_rsig, reads=[r_po], writes=[r_ewA])

                        def em_vmm(e, tt=tt, db=db):
                            last = None
                            for dc in range(NCH):
                                last = e.matmul(
                                    out=pq_ps[:, 0:512],
                                    lhsT=kactT[:, dc * 512 + tt * 128:dc * 512 + (tt + 1) * 128],
                                    rhs=wv2_bf[:, dc * D + db * 512:dc * D + (db + 1) * 512],
                                    start=(dc == 0), stop=(dc == NCH - 1))
                            return last
                        p.op("tensor", em_vmm, reads=[r_watt, r_V], writes=[r_pq])

                        def em_y(e, db=db):
                            return e.tensor_tensor(
                                out=y0[:, db * 512:(db + 1) * 512],
                                in0=pq_ps[:, 0:512],
                                in1=r_ffn_sb[:, db * 512:(db + 1) * 512],
                                op=ALU.mult)
                        p.op("vector", em_y, reads=[r_pq, r_ewA], writes=[r_y0])
                    emit_tail(s, tc, src_d, dst_d)

        def emit_l1attn(s, src_d, dst_d):
            emit_preln(s, src_d, first_block=False)
            for wd, scale_q, dst, r_dst in ((w1qT_d, True, QT, r_QT),
                                            (w1kT_d, False, KT, r_KT)):
                load_w(wd)
                for dc in range(NCH):
                    for tb in range(4):
                        def em_mm(e, dc=dc, tb=tb):
                            last = None
                            for cc in range(NCH):
                                last = e.matmul(
                                    out=pq_ps[:, 0:512],
                                    lhsT=w_bf[:, cc * D + dc * 128:cc * D + (dc + 1) * 128],
                                    rhs=xnT[:, cc * T + tb * 512:cc * T + (tb + 1) * 512],
                                    start=(cc == 0), stop=(cc == NCH - 1))
                            return last
                        p.op("tensor", em_mm, reads=[r_wbf, r_xnT], writes=[r_pq])

                        def em_ev(e, dc=dc, tb=tb, scale_q=scale_q, dst=dst):
                            sc1 = 0.125 if scale_q else 1.0
                            return e.tensor_scalar(
                                out=dst[:, dc * T + tb * 512:dc * T + (tb + 1) * 512],
                                in0=pq_ps[:, 0:512], scalar1=sc1, scalar2=0.0,
                                op0=ALU.mult, op1=ALU.add)
                        p.op("vector", em_ev, reads=[r_pq], writes=[r_dst])

            load_w(w1vT_d)
            for tc in range(NTC):
                for db in range(2):
                    def em_mm(e, tc=tc, db=db):
                        last = None
                        for cc in range(NCH):
                            last = e.matmul(
                                out=pq_ps[:, 0:512],
                                lhsT=xnT[:, cc * T + tc * 128:cc * T + (tc + 1) * 128],
                                rhs=w_bf[:, cc * D + db * 512:cc * D + (db + 1) * 512],
                                start=(cc == 0), stop=(cc == NCH - 1))
                        return last
                    p.op("tensor", em_mm, reads=[r_xnT, r_wbf], writes=[r_pq])

                    def em_ev(e, tc=tc, db=db):
                        return e.tensor_copy(
                            V[:, tc * D + db * 512:tc * D + (db + 1) * 512],
                            pq_ps[:, 0:512])
                    p.op("vector", em_ev, reads=[r_pq], writes=[r_V])

            load_w(w1oT_d)

            for w in range(NW):
                for pr in range(8):
                    for h in range(2):
                        hg = 2 * pr + h
                        def em_s(e, pr=pr, h=h, w=w):
                            e.matmul(out=sim[h][:, 0:64],
                                     lhsT=QT[h * 64:(h + 1) * 64,
                                             pr * T + w * 128:pr * T + (w + 1) * 128],
                                     rhs=KT[h * 64:(h + 1) * 64, pr * T:pr * T + 64],
                                     start=True, stop=True, tile_position=(h * 64, 0))
                            return e.matmul(out=sim[h][:, 64:192],
                                            lhsT=QT[h * 64:(h + 1) * 64,
                                                    pr * T + w * 128:pr * T + (w + 1) * 128],
                                            rhs=KT[h * 64:(h + 1) * 64,
                                                   pr * T + w * 128:pr * T + (w + 1) * 128],
                                            start=True, stop=True,
                                            tile_position=(h * 64, 0))
                        p.op("tensor", em_s, reads=[r_QT, r_KT], writes=[r_sim[h]])

                        def em_mx(e, h=h):
                            e.tensor_reduce(out=small[:, 4:5], in_=sim[h][:, 0:192],
                                            axis=mybir.AxisListType.X, op=ALU.max)
                            e.drain(fusable=False)
                            return e.tensor_scalar(out=small[:, 5:6], in0=small[:, 4:5],
                                                   scalar1=-1.0, scalar2=0.0,
                                                   op0=ALU.mult, op1=ALU.add)
                        p.op("vector", em_mx, reads=[r_sim[h]], writes=[r_small])

                        def em_exp(e, h=h):
                            return e.activation(out=p_raw, in_=sim[h][:, 0:192],
                                                func=AF.Exp, scale=1.0,
                                                bias=small[:, 5:6],
                                                accum_out=small[:, 6:7])
                        p.op("scalar", em_exp, reads=[r_sim[h], r_small],
                             writes=[r_watt, r_small])

                        p_nrm = ewB[:, 0:192]

                        def em_pn(e, p_nrm=p_nrm):
                            e.reciprocal(out=small[:, 4:5], in_=small[:, 6:7])
                            e.drain(fusable=False)
                            return e.tensor_scalar(out=p_nrm, in0=p_raw,
                                                   scalar1=small[:, 4:5], scalar2=0.0,
                                                   op0=ALU.mult, op1=ALU.add)
                        p.op("vector", em_pn, reads=[r_watt, r_small],
                             writes=[r_ewB, r_small])
                        dve_flush(extra_writes=[r_ewB])

                        def em_tp(e, p_nrm=p_nrm):
                            e.transpose(out=nm_ps[0:64, 0:128],
                                        in_=p_nrm[:, 0:64],
                                        identity=eye_f[:])
                            return e.transpose(out=nm_ps[:, 128:256],
                                              in_=p_nrm[:, 64:192],
                                              identity=eye_f[:])
                        p.op("tensor", em_tp, reads=[r_ewB, r_eye], writes=[r_nm])

                        def em_tpe(e):
                            e.tensor_copy(pT_g[0:64, 0:128], nm_ps[0:64, 0:128])
                            return e.tensor_copy(pT_w[:, 0:128], nm_ps[:, 128:256])
                        p.op("vector", em_tpe, reads=[r_nm], writes=[r_watt])

                        def em_avw(e, pr=pr, h=h, w=w, hg=hg):
                            e.matmul(out=av_ps[h * 64:(h + 1) * 64, 0:128],
                                     lhsT=V[0:64, hg * 64:(hg + 1) * 64],
                                     rhs=pT_g[0:64, 0:128],
                                     start=True, stop=False,
                                     tile_position=(0, h * 64))
                            return e.matmul(out=av_ps[h * 64:(h + 1) * 64, 0:128],
                                            lhsT=V[:, w * D + hg * 64:w * D + (hg + 1) * 64],
                                            rhs=pT_w[:, 0:128],
                                            start=False, stop=True,
                                            tile_position=(0, h * 64))
                        p.op("tensor", em_avw, reads=[r_V, r_watt], writes=[r_av])

                    def em_dv(e, pr=pr):
                        return e.tensor_copy(attnT[:, pr * 128:(pr + 1) * 128],
                                             av_ps[:, 0:128])
                    p.op("vector", em_dv, reads=[r_av], writes=[r_attnT])

                for db in range(2):
                    def em_mm(e, w=w, db=db):
                        last = None
                        for pp in range(8):
                            last = e.matmul(out=po_ps[:, 0:512],
                                            lhsT=attnT[:, pp * 128:(pp + 1) * 128],
                                            rhs=w_bf[:, pp * D + db * 512:
                                                     pp * D + (db + 1) * 512],
                                            start=(pp == 0), stop=(pp == 7))
                        return last
                    p.op("tensor", em_mm, reads=[r_attnT, r_wbf], writes=[r_po])

                    def em_ev(e, db=db):
                        return e.tensor_copy(y0[:, db * 512:(db + 1) * 512],
                                             po_ps[:, 0:512])
                    p.op("vector", em_ev, reads=[r_po], writes=[r_y0])
                emit_tail(s, w, src_d, dst_d)

        import os as _os
        _nseq = int(_os.environ.get("KSEQS", str(SEQ_PER_CORE)))
        _blocks = _os.environ.get("KBLOCKS", "afwg")
        _chain = [x_d, h1_d, h2_d, h3_d, out_d]
        _src = 0
        for _bi, _b in enumerate(_blocks):
            _dst = 4 if _bi == len(_blocks) - 1 else _src + 1
            for s in range(_nseq):
                if _b == "a":
                    emit_l0attn(s, _chain[_src], _chain[_dst])
                elif _b == "f":
                    emit_ffn(s, _chain[_src], _chain[_dst], f0r_d, f0k_d, f0v_d, 0)
                elif _b == "g":
                    emit_ffn(s, _chain[_src], _chain[_dst], f1r_d, f1k_d, f1v_d, 2)
                elif _b == "w":
                    emit_l1attn(s, _chain[_src], _chain[_dst])
            _src = _dst

        p.wait_only("gpsimd", [("dout", p.counts["dout"])])
        p.build_block()
    return nc


def _rep128(v):
    a = np.asarray(v, np.float32).reshape(1, -1)
    return np.ascontiguousarray(np.broadcast_to(a, (128, a.shape[1])))


def _part8(v):
    return np.ascontiguousarray(np.asarray(v, np.float32).reshape(8, 128).T)


def _tr(w):
    return np.ascontiguousarray(np.asarray(w, np.float32).T)


def module_in_maps(inputs, x):
    eye = np.eye(128, dtype=np.float32).astype(ml_dtypes.bfloat16)
    mu_p = np.concatenate([_part8(inputs["L0_ffn_mu_r"]), _part8(inputs["L0_ffn_mu_k"]),
                           _part8(inputs["L1_ffn_mu_r"]), _part8(inputs["L1_ffn_mu_k"])],
                          axis=1)
    shared = dict(
        wqT=_tr(inputs["L0_attn_Wq"]), wkT=_tr(inputs["L0_attn_Wk"]),
        wvT=_tr(inputs["L0_attn_Wv"]), woT=_tr(inputs["L0_attn_Wo"]),
        bq_p=_part8(inputs["L0_attn_bq"]), bk_p=_part8(inputs["L0_attn_bk"]),
        bv_row=np.asarray(inputs["L0_attn_bv"], np.float32).reshape(1, -1),
        bo_row=np.asarray(inputs["L0_attn_bo"], np.float32).reshape(1, -1),
        alpha_t=_rep128(np.asarray(inputs["L0_attn_alpha"], np.float32).reshape(-1)),
        phi_t=_rep128(np.asarray(inputs["L0_attn_phi"], np.float32).reshape(-1)),
        temp_t=_rep128(np.asarray(inputs["L0_attn_temperature"], np.float32)),
        gamma_t=_rep128(inputs["L0_attn_gamma"]),
        beta_t=_rep128(inputs["L0_attn_beta"]),
        eye_bf=eye,
        eye_f=np.eye(128, dtype=np.float32),
        f0rT=_tr(inputs["L0_ffn_W_r"]), f0kT=_tr(inputs["L0_ffn_W_k"]),
        f0vT=_tr(inputs["L0_ffn_W_v"]),
        f1rT=_tr(inputs["L1_ffn_W_r"]), f1kT=_tr(inputs["L1_ffn_W_k"]),
        f1vT=_tr(inputs["L1_ffn_W_v"]),
        mu_p=mu_p,
        w1qT=_tr(inputs["L1_attn_Wq"]), w1kT=_tr(inputs["L1_attn_Wk"]),
        w1vT=_tr(inputs["L1_attn_Wv"]), w1oT=_tr(inputs["L1_attn_Wo"]),
    )
    maps = []
    for c in range(NCORES):
        m = dict(shared)
        m["x"] = np.ascontiguousarray(x[c * SEQ_PER_CORE:(c + 1) * SEQ_PER_CORE])
        maps.append(m)
    return maps


def kernel(**inputs):
    x = np.asarray(inputs["L0_attn_x"], np.float32)
    nc = build_module()
    res = run_bass_kernel_spmd(nc, module_in_maps(inputs, x),
                               core_ids=list(range(NCORES)))
    return np.concatenate([np.asarray(r["out"]) for r in res.results], axis=0)
